# version 04
# baseline (speedup 1.0000x reference)
import numpy as np
import contextlib
import concourse.bass as bass
import concourse.mybir as mybir
from concourse.bass_utils import run_bass_kernel_spmd

F32 = mybir.dt.float32
BF16 = mybir.dt.bfloat16
A = mybir.AluOpType
AF = mybir.ActivationFunctionType
AXX = mybir.AxisListType.X

C = 1024
NCB = 8
TC = 512
JC = TC // 128
TT = 1024
JT = TT // 128
NE = 9
ALPHA = 1.702
LIMIT = 7.0
LNEPS = 1e-5
TOPK_BIG = 1e30

ENGS = ("sync", "gpsimd", "tensor", "vector", "scalar")
V_TMK, V_TMV, V_TMR, V_TD, V_TF, V_LNG, V_LNB = range(7)


def _flat(ap):
    n = 1
    for _, cnt in ap.ap[1:]:
        n *= cnt
    return bass.AP(ap.tensor, ap.offset, [list(ap.ap[0]), [1, n]])


class Prog:

    def __init__(self, sems):
        self.ops = {e: [] for e in ENGS}
        self.sems = sems
        self.cnt = {k: 0 for k in sems}

    def wait(self, eng, sem, val):
        if val <= 0:
            return
        assert val <= self.cnt[sem], (sem, val, self.cnt[sem])
        self.ops[eng].append(("wait", self.sems[sem], val))

    def call(self, eng, meth, *args, inc=None, **kw):
        incs = []
        if inc:
            for s, n in inc:
                self.cnt[s] += n
                incs.append((self.sems[s], n))
        self.ops[eng].append(("call", meth, args, kw, incs))
        if eng in ("vector", "scalar") and meth not in ("drain", "dma_start"):
            self.ops[eng].append(("call", "drain", (), {}, []))

    def dma(self, eng, sem, out, in_):
        self.wait(eng, sem, self.cnt[sem])
        self.call(eng, "dma_start", inc=[(sem, 16)], out=out, in_=in_)
        return self.cnt[sem]

    def flush(self, block):
        def run(e, ops):
            for op in ops:
                if op[0] == "wait":
                    e.wait_ge(op[1], op[2])
                else:
                    _, meth, args, kw, incs = op
                    ins = getattr(e, meth)(*args, **kw)
                    for s, n in incs:
                        ins = ins.then_inc(s, n)
        for name in ENGS:
            ops = self.ops[name]
            if ops:
                getattr(block, name)(lambda e, _o=ops: run(e, _o))
        self.ops = {e: [] for e in ENGS}

    def barrier_vals(self):
        return dict(self.cnt)

    def emit_barrier(self, vals):
        for e in ENGS:
            for s, v in vals.items():
                if v > 0:
                    self.wait(e, s, v)


def build_nc(T=4096):
    NCH = T // TC
    NTL = T // TT
    nc = bass.Bass("TRN2", target_bir_lowering=False, debug=False)

    dt_ = nc.dram_tensor
    d_x = dt_("x", [T, C], F32, kind="ExternalInput").ap()
    d_out = dt_("out", [T, C], F32, kind="ExternalOutput").ap()
    d_aw = dt_("attn_w", [4, 8, 128, NCB, 128], F32, kind="ExternalInput").ap()
    d_fw = dt_("ffn_w", [NE, 4, 8, 128, NCB, 128], F32, kind="ExternalInput").ap()
    d_fw2 = dt_("ffn_w2", [NE, 4, 128, 8, 2, 128], F32, kind="ExternalInput").ap()
    d_gw = dt_("gw", [128, NCB, 8], F32, kind="ExternalInput").ap()
    d_vec = dt_("vec", [128, 7, NCB], F32, kind="ExternalInput").ap()
    d_bias = dt_("bias", [128, NE, 5, 8], F32, kind="ExternalInput").ap()
    d_gb = dt_("gb", [1, 8], F32, kind="ExternalInput").ap()
    d_id = dt_("ident", [128, 128], F32, kind="ExternalInput").ap()
    d_or = dt_("ones_row", [1, 128], F32, kind="ExternalInput").ap()
    d_oc = dt_("ones_col", [128, 1], F32, kind="ExternalInput").ap()
    d_ac = dt_("actc", [128, 4], F32, kind="ExternalInput").ap()
    d_sel = dt_("sel8", [8, 8, 128], F32, kind="ExternalInput").ap()
    s_aw = dt_("aw_s", [4, 8, 128, 2, NCB, 128], BF16, kind="Internal").ap()
    s_fw = dt_("fw_s", [NE, 8, 4, 128, 2, NCB, 128], BF16, kind="Internal").ap()
    s_fw2 = dt_("fw2_s", [NE, 4, 128, 2, 8, 2, 128], BF16, kind="Internal").ap()
    s_xp = dt_("xpost", [T, C], F32, kind="Internal").ap()

    with contextlib.ExitStack() as ST:
        sem_names = ["sx0", "sx1", "sw0", "sw1", "sv", "ss0", "ss1", "st",
                     "pe", "ve", "se", "gp", "sp0", "sp1", "sq0", "sq1"]
        sems = {k: ST.enter_context(nc.semaphore(k)) for k in sem_names}
        P = Prog(sems)

        ident = ST.enter_context(nc.sbuf_tensor("identS", [128, 128], F32))
        ones_row = ST.enter_context(nc.sbuf_tensor("ones_rowS", [1, 128], F32))
        ones_col = ST.enter_context(nc.sbuf_tensor("ones_colS", [128, 1], F32))
        actc = ST.enter_context(nc.sbuf_tensor("actcS", [128, 4], F32))
        sel8 = ST.enter_context(nc.sbuf_tensor("sel8S", [8, 8, 128], F32))
        nc.const_aps.register(F32, LNEPS, actc[:, 0:1])
        nc.const_aps.register(F32, -0.5, actc[:, 1:2])
        nc.const_aps.register(F32, ALPHA, actc[:, 2:3])
        nc.const_aps.register(F32, 1.0 / C, actc[:, 3:4])
        vecs = ST.enter_context(nc.sbuf_tensor("vecs", [128, 7, NCB], F32))
        dec_sb = ST.enter_context(nc.sbuf_tensor("dec_sb", [128, NCB], F32))
        gws = ST.enter_context(nc.sbuf_tensor("gws", [128, NCB, 8], F32))
        gwh = ST.enter_context(nc.sbuf_tensor("gwh", [128, NCB, 8], BF16))
        gwl = ST.enter_context(nc.sbuf_tensor("gwl", [128, NCB, 8], BF16))
        gbt = ST.enter_context(nc.sbuf_tensor("gbt", [128, 8], F32))
        gbrow = ST.enter_context(nc.sbuf_tensor("gbrow", [1, 8], F32))
        bvec = ST.enter_context(nc.sbuf_tensor("bvec", [128, NE, 5, 8], F32))
        carry = ST.enter_context(nc.sbuf_tensor("carry", [128, NCB], F32))
        pb = [ST.enter_context(nc.psum_tensor("pb%d" % i, [128, 512], F32))
              for i in range(8)]

        slot_state = {"A": None, "B": None, "next": "A"}

        def slot_acquire():
            s = slot_state["next"]
            slot_state["next"] = "B" if s == "A" else "A"
            return s, (pb[0:4] if s == "A" else pb[4:8])

        def slot_wait(s):
            ev = slot_state[s]
            if ev is not None:
                P.wait("tensor", ev[0], ev[1])

        def slot_done(s, eng="ve"):
            slot_state[s] = (eng, P.cnt[eng])

        P.dma("sync", "sv", ident[:], d_id[:, :])
        P.dma("sync", "sv", ones_row[:], d_or[:, :])
        P.dma("sync", "sv", ones_col[:], d_oc[:, :])
        P.dma("sync", "sv", actc[:], d_ac[:, :])
        P.dma("sync", "sv", sel8[:], d_sel[:, :, :])
        P.dma("sync", "sv", vecs[:], d_vec[:, :, :])
        P.dma("sync", "sv", gws[:], d_gw[:, :, :])
        P.dma("sync", "sv", gbrow[:], d_gb[:, :])
        c_setup = P.dma("sync", "sv", bvec[:], d_bias[:, :, :, :])
        P.wait("scalar", "sv", c_setup)
        P.call("scalar", "activation", dec_sb[:], vecs[:, V_TD, :], AF.Exp,
               inc=[("se", 1)])
        P.wait("vector", "sv", c_setup)
        P.call("vector", "tensor_copy", gwh[:], gws[:])
        P.call("vector", "tensor_sub", gwl[:], gws[:], gwh[:], inc=[("ve", 1)])
        P.wait("tensor", "sv", c_setup)
        P.call("tensor", "matmul", pb[0][0:128, 0:8], ones_row[:], gbrow[:, :],
               start=True, stop=True, inc=[("pe", 1)])
        P.wait("vector", "pe", P.cnt["pe"])
        P.call("vector", "tensor_copy", gbt[:], pb[0][0:128, 0:8], inc=[("ve", 1)])
        c_gbt = P.cnt["ve"]

        p0stack = contextlib.ExitStack()
        p0scope = p0stack
        p0t = p0scope.enter_context(nc.sbuf_tensor("p0_t", [128, 2, 1024], F32))
        p0h = p0scope.enter_context(nc.sbuf_tensor("p0_h", [128, 2, 1024], BF16))
        p0l = p0scope.enter_context(nc.sbuf_tensor("p0_l", [128, 2, 1024], BF16))

        def half_ap(ap, off, n):
            return bass.AP(ap.tensor, ap.offset + off, [list(ap.ap[0]), [1, n]])

        slabs_a = []
        for pj in range(4):
            for db in range(8):
                slabs_a.append((_flat(d_aw[pj, db]), _flat(s_aw[pj, db, :, 0]),
                                _flat(s_aw[pj, db, :, 1])))
        slabs_f = []
        for e in range(NE):
            for pa in range(4):
                for fb in range(8):
                    slabs_f.append((_flat(d_fw[e, pa, fb]),
                                    _flat(s_fw[e, fb, pa, :, 0]),
                                    _flat(s_fw[e, fb, pa, :, 1])))
            for dp in range(4):
                for hh in range(2):
                    slabs_f.append(
                        (half_ap(_flat(d_fw2[e, dp]), hh * 1024, 1024),
                         half_ap(_flat(s_fw2[e, dp, :, 0]), hh * 1024, 1024),
                         half_ap(_flat(s_fw2[e, dp, :, 1]), hh * 1024, 1024)))
        p0_state = {"i": 0, "cast": [0, 0], "store": [0, 0]}

        def emit_slab(src, dh, dl):
            i = p0_state["i"]
            p0_state["i"] += 1
            b = i % 2
            lsem = ("sp0", "sp1")[b]
            ssem = ("sq0", "sq1")[b]
            n = 1024
            if p0_state["cast"][b]:
                P.wait("gpsimd", "gp", p0_state["cast"][b])
            ld = P.dma("gpsimd", lsem, p0t[:, b, 0:n], src)
            P.wait("scalar", lsem, ld)
            if p0_state["store"][b]:
                P.wait("scalar", ssem, p0_state["store"][b])
            P.call("scalar", "copy", p0h[:, b, 0:n], p0t[:, b, 0:n],
                   inc=[("se", 1)])
            hic = P.cnt["se"]
            P.wait("gpsimd", "se", hic)
            if p0_state["store"][b]:
                P.wait("gpsimd", ssem, p0_state["store"][b])
            P.call("gpsimd", "tensor_sub", p0l[:, b, 0:n], p0t[:, b, 0:n],
                   p0h[:, b, 0:n], inc=[("gp", 1)])
            p0_state["cast"][b] = P.cnt["gp"]
            P.wait("gpsimd", "gp", p0_state["cast"][b])
            P.wait("gpsimd", ssem, P.cnt[ssem])
            P.call("gpsimd", "dma_start", inc=[(ssem, 16)], out=dh,
                   in_=p0h[:, b, 0:n])
            P.call("gpsimd", "dma_start", inc=[(ssem, 16)], out=dl,
                   in_=p0l[:, b, 0:n])
            p0_state["store"][b] = P.cnt[ssem]

        for sl in slabs_a:
            emit_slab(*sl)
        bar0 = P.barrier_vals()

        with contextlib.ExitStack() as SB:
            _t = lambda n, sh, dt: SB.enter_context(nc.sbuf_tensor(n, sh, dt))
            x_tm = _t("x_tm", [128, 2, JC, 1024], F32)
            sq = _t("sq", [128, 1024], F32)
            xn_fm = _t("xn_fm", [128, NCB, TC + 1], F32)
            dxm = _t("dxm", [128, NCB, TC], F32)
            xmix = _t("xmix", [128, NCB, TC], F32)
            mxh = _t("mxh", [128, NCB, TC], BF16)
            mxl = _t("mxl", [128, NCB, TC], BF16)
            k_fm = _t("k_fm", [128, NCB, TC], F32)
            v_fm = _t("v_fm", [128, NCB, TC], F32)
            r_fm = _t("r_fm", [128, NCB, TC], F32)
            t_fm = _t("t_fm", [128, NCB, TC], F32)
            stat = _t("stat", [128, 3], F32)
            rows = _t("rows", [1, 4, 512], F32)
            rb = _t("rb", [128, 512], F32)
            mrb = _t("mrb", [128, 512], F32)
            wsl = _t("wsl", [128, 2, 2, NCB, 128], BF16)
            P.emit_barrier(bar0)
            P.call("vector", "memset", xn_fm[:, :, 0:1], 0.0)
            P.call("vector", "memset", carry[:], 0.0, inc=[("ve", 1)])

            xld = [0, 0]
            xst = [0, 0]
            resid_done = [0, 0]
            wcons = [0, 0]
            slab_i = 0
            dxm_prev = 0
            ROW_M, ROW_MSQ, ROW_RSTD, ROW_MR = 0, 1, 2, 3

            for ch in range(NCH):
                cb_ = ch % 2
                xsem = ("sx0", "sx1")[cb_]
                ssem = ("ss0", "ss1")[cb_]
                P.wait("vector", "pe", P.cnt["pe"])
                if ch > 0:
                    P.wait("vector", "gp", dxm_prev)
                    P.call("vector", "tensor_copy", xn_fm[:, :, 0:1],
                           xn_fm[:, :, TC:TC + 1], inc=[("ve", 1)])
                if resid_done[cb_]:
                    P.wait("sync", "ve", resid_done[cb_])
                if xst[cb_]:
                    P.wait("sync", ssem, xst[cb_])
                xld[cb_] = P.dma("sync", xsem, x_tm[:, cb_, :, :],
                                 d_x[ch * TC:(ch + 1) * TC, :].rearrange(
                                     "(j p) c -> p j c", p=128))
                P.wait("vector", xsem, xld[cb_])
                P.wait("scalar", xsem, xld[cb_])
                tr_done_pe = 0
                halo_ve = P.cnt["ve"]
                for j in range(JC):
                    xt = x_tm[:, cb_, j, :]
                    P.call("vector", "reduce_sum", stat[:, 0:1], xt, AXX)
                    P.call("vector", "tensor_scalar", stat[:, 0:1], stat[:, 0:1],
                           -1.0 / C, None, A.mult, inc=[("ve", 1)])
                    if j > 0:
                        P.wait("scalar", "pe", tr_done_pe)
                    P.wait("scalar", "ve", P.cnt["ve"])
                    P.call("scalar", "activation", sq[:], xt, AF.Square,
                           bias=stat[:, 0:1], accum_out=stat[:, 1:2])
                    P.call("scalar", "activation", stat[:, 1:2], stat[:, 1:2],
                           AF.Ln, bias=LNEPS, scale=1.0 / C)
                    P.call("scalar", "activation", stat[:, 1:2], stat[:, 1:2],
                           AF.Exp, scale=-0.5, inc=[("se", 1)])
                    P.wait("vector", "se", P.cnt["se"])
                    P.call("vector", "tensor_mul", stat[:, 2:3], stat[:, 0:1],
                           stat[:, 1:2], inc=[("ve", 1)])
                    P.wait("scalar", "ve", P.cnt["ve"])
                    P.call("scalar", "activation", sq[:], xt, AF.Identity,
                           bias=stat[:, 2:3], scale=stat[:, 1:2],
                           inc=[("se", 1)])
                    s_, banks = slot_acquire()
                    slot_wait(s_)
                    P.wait("tensor", "se", P.cnt["se"])
                    if ch > 0 and j == 0:
                        P.wait("tensor", "gp", dxm_prev)
                        P.wait("tensor", "ve", halo_ve)
                    for cb in range(NCB):
                        P.call("tensor", "transpose",
                               banks[cb % 4][:, (cb // 4) * 128:(cb // 4) * 128 + 128],
                               sq[:, cb * 128:(cb + 1) * 128], ident[:],
                               inc=[("pe", 1)] if cb == NCB - 1 else None)
                    tr_done_pe = P.cnt["pe"]
                    P.wait("scalar", "pe", tr_done_pe)
                    for cb in range(NCB):
                        P.call("scalar", "copy",
                               xn_fm[:, cb, 1 + j * 128:1 + (j + 1) * 128],
                               banks[cb % 4][:, (cb // 4) * 128:(cb // 4) * 128 + 128],
                               inc=[("se", 1)] if cb == NCB - 1 else None)
                    slot_done(s_, "se")
                xn_evac_se = P.cnt["se"]
                P.wait("gpsimd", "se", xn_evac_se)
                P.wait("gpsimd", "ve", P.cnt["ve"])
                P.call("gpsimd", "tensor_sub", dxm[:], xn_fm[:, :, 1:TC + 1],
                       xn_fm[:, :, 0:TC], inc=[("gp", 1)])
                dxm_prev = P.cnt["gp"]

                ev_se = {}

                def mm_proj(pj, dest):
                    nonlocal slab_i
                    for half in range(2):
                        s_, banks = slot_acquire()
                        slot_wait(s_)
                        P.wait("tensor", "ve", P.cnt["ve"])
                        for q in range(4):
                            db = half * 4 + q
                            wb = slab_i % 2
                            wsem = ("sw0", "sw1")[wb]
                            if wcons[wb]:
                                P.wait("sync", "pe", wcons[wb])
                            wld = P.dma("sync", wsem, wsl[:, wb], s_aw[pj, db])
                            P.wait("tensor", wsem, wld)
                            nmm = 0
                            for pss in range(3):
                                mv = (mxh, mxl, mxh)[pss]
                                hl = (0, 0, 1)[pss]
                                for cb in range(NCB):
                                    nmm += 1
                                    P.call("tensor", "matmul", banks[q][:, :],
                                           wsl[:, wb, hl, cb, :], mv[:, cb, :],
                                           start=(nmm == 1), stop=(nmm == 24),
                                           inc=[("pe", 1)] if nmm == 24 else None)
                            wcons[wb] = P.cnt["pe"]
                            slab_i += 1
                        P.wait("scalar", "pe", P.cnt["pe"])
                        for q in range(4):
                            P.call("scalar", "copy",
                                   dest[:, half * 4 + q, :], banks[q][:, :],
                                   inc=[("se", 1)] if q == 3 else None)
                        slot_done(s_, "se")
                    ev_se[pj] = P.cnt["se"]

                for pj, dest in ((0, k_fm), (1, v_fm), (2, r_fm)):
                    P.wait("vector", "pe", P.cnt["pe"])
                    if pj == 0:
                        P.wait("vector", "gp", dxm_prev)
                    for cb in range(NCB):
                        P.call("vector", "scalar_tensor_tensor",
                               xmix[:, cb, :], dxm[:, cb, :],
                               vecs[:, pj, cb:cb + 1], xn_fm[:, cb, 0:TC],
                               A.mult, A.add)
                    P.call("vector", "tensor_copy", mxh[:], xmix[:])
                    P.call("vector", "tensor_sub", mxl[:], xmix[:], mxh[:],
                           inc=[("ve", 1)])
                    mm_proj(pj, dest)

                P.wait("gpsimd", "se", ev_se[1])
                P.call("gpsimd", "tensor_mul", k_fm[:], k_fm[:], v_fm[:],
                       inc=[("gp", 1)])
                kv_gp = P.cnt["gp"]
                P.wait("vector", "gp", kv_gp)
                for cb in range(NCB):
                    dbc = bass.AP(dec_sb, cb, [[NCB, 128], [0, TC]])
                    P.call("vector", "tensor_tensor_scan", v_fm[:, cb, :],
                           dbc, k_fm[:, cb, :], carry[:, cb:cb + 1],
                           A.mult, A.add)
                P.call("vector", "tensor_copy", carry[:], v_fm[:, :, TC - 1])
                for cb in range(NCB):
                    P.call("vector", "scalar_tensor_tensor", t_fm[:, cb, :],
                           k_fm[:, cb, :], vecs[:, V_TF, cb:cb + 1],
                           v_fm[:, cb, :], A.mult, A.add)
                P.wait("vector", "se", ev_se[2])
                P.call("vector", "tensor_mul", t_fm[:], r_fm[:], t_fm[:],
                       inc=[("ve", 1)])
                o_done = P.cnt["ve"]

                P.wait("scalar", "ve", o_done)
                P.call("scalar", "activation", k_fm[:], t_fm[:], AF.Square,
                       inc=[("se", 1)])
                s_, banks = slot_acquire()
                slot_wait(s_)
                P.wait("tensor", "ve", o_done)
                P.wait("tensor", "se", P.cnt["se"])
                for cb in range(NCB):
                    P.call("tensor", "matmul", banks[0][0:1, :], ones_col[:],
                           t_fm[:, cb, :], start=(cb == 0), stop=(cb == NCB - 1),
                           inc=[("pe", 1)] if cb == NCB - 1 else None)
                for cb in range(NCB):
                    P.call("tensor", "matmul", banks[1][0:1, :], ones_col[:],
                           k_fm[:, cb, :], start=(cb == 0), stop=(cb == NCB - 1),
                           inc=[("pe", 1)] if cb == NCB - 1 else None)
                P.wait("vector", "pe", P.cnt["pe"])
                P.call("vector", "tensor_copy", rows[:, ROW_M, :], banks[0][0:1, :])
                P.call("vector", "tensor_copy", rows[:, ROW_MSQ, :],
                       banks[1][0:1, :], inc=[("ve", 1)])
                slot_done(s_)
                P.call("vector", "tensor_scalar", rows[:, ROW_M, :],
                       rows[:, ROW_M, :], 1.0 / C, None, A.mult)
                P.call("vector", "tensor_mul", rows[:, ROW_RSTD, :],
                       rows[:, ROW_M, :], rows[:, ROW_M, :])
                P.call("vector", "scalar_tensor_tensor", rows[:, ROW_MSQ, :],
                       rows[:, ROW_MSQ, :], 1.0 / C, rows[:, ROW_RSTD, :],
                       A.mult, A.subtract, inc=[("ve", 1)])
                P.wait("scalar", "ve", P.cnt["ve"])
                P.call("scalar", "activation", rows[:, ROW_RSTD, :],
                       rows[:, ROW_MSQ, :], AF.Ln, bias=LNEPS)
                P.call("scalar", "activation", rows[:, ROW_RSTD, :],
                       rows[:, ROW_RSTD, :], AF.Exp, scale=-0.5,
                       inc=[("se", 1)])
                P.wait("vector", "se", P.cnt["se"])
                P.call("vector", "tensor_mul", rows[:, ROW_MR, :],
                       rows[:, ROW_M, :], rows[:, ROW_RSTD, :],
                       inc=[("ve", 1)])
                rows_done = P.cnt["ve"]
                s_, banks = slot_acquire()
                slot_wait(s_)
                P.wait("tensor", "ve", rows_done)
                P.call("tensor", "matmul", banks[0][:, :], ones_row[:],
                       rows[:, ROW_RSTD, :], start=True, stop=True)
                P.call("tensor", "matmul", banks[1][:, :], ones_row[:],
                       rows[:, ROW_MR, :], start=True, stop=True,
                       inc=[("pe", 1)])
                P.wait("scalar", "pe", P.cnt["pe"])
                P.call("scalar", "copy", rb[:], banks[0][:, :])
                P.call("scalar", "copy", mrb[:], banks[1][:, :],
                       inc=[("se", 1)])
                slot_done(s_, "se")
                rbmrb_se = P.cnt["se"]
                P.wait("vector", "pe", P.cnt["pe"])
                P.wait("vector", "se", rbmrb_se)
                for cb in range(NCB):
                    P.call("vector", "tensor_mul", xmix[:, cb, :],
                           t_fm[:, cb, :], rb[:])
                    P.call("vector", "tensor_sub", xmix[:, cb, :],
                           xmix[:, cb, :], mrb[:],
                           inc=[("ve", 1)] if cb == NCB - 1 else None)
                P.wait("scalar", "ve", P.cnt["ve"])
                P.wait("scalar", "pe", P.cnt["pe"])
                for cb in range(NCB):
                    P.call("scalar", "activation", xmix[:, cb, :],
                           xmix[:, cb, :], AF.Identity,
                           bias=vecs[:, V_LNB, cb:cb + 1],
                           scale=vecs[:, V_LNG, cb:cb + 1])
                P.call("scalar", "copy", mxh[:], xmix[:], inc=[("se", 1)])
                P.wait("vector", "se", P.cnt["se"])
                P.call("vector", "tensor_sub", mxl[:], xmix[:], mxh[:],
                       inc=[("ve", 1)])
                mm_proj(3, t_fm)
                ao_se = ev_se[3]
                for half in range(2):
                    s_, banks = slot_acquire()
                    slot_wait(s_)
                    P.wait("tensor", "se", ao_se)
                    for q in range(4):
                        db = half * 4 + q
                        for j in range(JC):
                            P.call("tensor", "transpose",
                                   banks[q][:, j * 128:(j + 1) * 128],
                                   t_fm[:, db, j * 128:(j + 1) * 128], ident[:],
                                   inc=[("pe", 1)] if (q == 3 and j == JC - 1) else None)
                    P.wait("vector", "pe", P.cnt["pe"])
                    for q in range(4):
                        db = half * 4 + q
                        for j in range(JC):
                            P.call("vector", "tensor_add",
                                   x_tm[:, cb_, j, db * 128:(db + 1) * 128],
                                   x_tm[:, cb_, j, db * 128:(db + 1) * 128],
                                   banks[q][:, j * 128:(j + 1) * 128],
                                   inc=[("ve", 1)] if (q == 3 and j == JC - 1) else None)
                    slot_done(s_)
                resid_done[cb_] = P.cnt["ve"]
                P.wait("gpsimd", "ve", resid_done[cb_])
                P.wait("gpsimd", ssem, P.cnt[ssem])
                P.call("gpsimd", "dma_start", inc=[(ssem, 16)],
                       out=s_xp[ch * TC:(ch + 1) * TC, :].rearrange(
                           "(j p) c -> p j c", p=128),
                       in_=x_tm[:, cb_, :, :])
                xst[cb_] = P.cnt[ssem]
                n_per = (len(slabs_f) + NCH - 1) // NCH
                for sl in slabs_f[ch * n_per:(ch + 1) * n_per]:
                    emit_slab(*sl)

            barA = P.barrier_vals()
            with nc.Block() as block:
                P.flush(block)

        p0stack.close()
        with contextlib.ExitStack() as SB:
            _t = lambda n, sh, dt: SB.enter_context(nc.sbuf_tensor(n, sh, dt))
            xp_tm = _t("xp_tm", [128, JT, 1024], F32)
            acc = _t("acc", [128, 8, TT], F32)
            x2h = _t("x2h", [128, NCB, TT], BF16)
            x2l = _t("x2l", [128, NCB, TT], BF16)
            ah = _t("ah", [128, 8, TT], BF16)
            al = _t("al", [128, 8, TT], BF16)
            wfs = _t("wfs", [128, 2, 4, 2, NCB, 128], BF16)
            w2s = _t("w2s", [128, 2, 2, 8, 2, 128], BF16)
            lnt = _t("lnt", [128, 1024], F32)
            t1 = _t("t1", [128, 1024], F32)
            t3 = _t("t3", [128, 1024], F32)
            cwb = _t("cwb", [128, TT], F32)
            lg = _t("lg", [128, TT], F32)
            gtm = _t("gtm", [128, JT, 8], F32)
            gst = _t("gst", [128, 4], F32)
            cwf = _t("cwf", [8, TT], F32)
            bst = _t("bst", [128, 3], F32)
            P.emit_barrier(barA)
            P.call("vector", "memset", lg[:], 0.0, inc=[("ve", 1)])

            out_st = 0
            wf_cons = [0, 0]
            w2_cons = [0, 0]

            for tl in range(NTL):
                P.wait("vector", "pe", P.cnt["pe"])
                if out_st:
                    P.wait("sync", "st", out_st)
                xpld = P.dma("sync", "sx0", xp_tm[:],
                             s_xp[tl * TT:(tl + 1) * TT, :].rearrange(
                                 "(j p) c -> p j c", p=128))
                P.wait("vector", "sx0", xpld)
                P.wait("scalar", "sx0", xpld)
                tr_done_pe = 0
                for j in range(JT):
                    xt = xp_tm[:, j, :]
                    P.call("vector", "reduce_sum", bst[:, 0:1], xt, AXX)
                    P.call("vector", "tensor_scalar", bst[:, 0:1], bst[:, 0:1],
                           -1.0 / C, None, A.mult, inc=[("ve", 1)])
                    if j > 0:
                        P.wait("scalar", "pe", tr_done_pe)
                    P.wait("scalar", "ve", P.cnt["ve"])
                    P.call("scalar", "activation", lnt[:], xt, AF.Square,
                           bias=bst[:, 0:1], accum_out=bst[:, 1:2])
                    P.call("scalar", "activation", bst[:, 1:2], bst[:, 1:2],
                           AF.Ln, bias=LNEPS, scale=1.0 / C)
                    P.call("scalar", "activation", bst[:, 1:2], bst[:, 1:2],
                           AF.Exp, scale=-0.5, inc=[("se", 1)])
                    P.wait("vector", "se", P.cnt["se"])
                    P.call("vector", "tensor_mul", bst[:, 2:3], bst[:, 0:1],
                           bst[:, 1:2], inc=[("ve", 1)])
                    P.wait("scalar", "ve", P.cnt["ve"])
                    P.call("scalar", "activation", lnt[:], xt, AF.Identity,
                           bias=bst[:, 2:3], scale=bst[:, 1:2],
                           inc=[("se", 1)])
                    s, banks = slot_acquire()
                    slot_wait(s)
                    P.wait("tensor", "se", P.cnt["se"])
                    for cb in range(NCB):
                        P.call("tensor", "transpose",
                               banks[cb % 4][:, (cb // 4) * 128:(cb // 4) * 128 + 128],
                               lnt[:, cb * 128:(cb + 1) * 128], ident[:],
                               inc=[("pe", 1)] if cb == NCB - 1 else None)
                    tr_done_pe = P.cnt["pe"]
                    P.wait("vector", "pe", tr_done_pe)
                    for cb in range(NCB):
                        src = banks[cb % 4][:, (cb // 4) * 128:(cb // 4) * 128 + 128]
                        dst = slice(j * 128, (j + 1) * 128)
                        P.call("vector", "tensor_copy", x2h[:, cb, dst], src)
                        P.call("vector", "tensor_sub", x2l[:, cb, dst], src,
                               x2h[:, cb, dst],
                               inc=[("ve", 1)] if cb == NCB - 1 else None)
                    slot_done(s)
                x2_done = P.cnt["ve"]

                s, banks = slot_acquire()
                slot_wait(s)
                P.wait("tensor", "ve", x2_done)
                for th in range(2):
                    tsl = slice(th * 512, (th + 1) * 512)
                    nmm = 0
                    for pss in range(3):
                        mvx = (x2h, x2l, x2h)[pss]
                        gwx = (gwh, gwh, gwl)[pss]
                        for cb in range(NCB):
                            nmm += 1
                            P.call("tensor", "matmul", banks[th][0:8, :],
                                   gwx[:, cb, :], mvx[:, cb, tsl],
                                   start=(nmm == 1), stop=(nmm == 24),
                                   inc=[("pe", 1)] if nmm == 24 else None)
                P.wait("vector", "pe", P.cnt["pe"])
                for th in range(2):
                    P.call("vector", "tensor_copy",
                           lg[0:8, th * 512:(th + 1) * 512], banks[th][0:8, :],
                           inc=[("ve", 1)] if th == 1 else None)
                slot_done(s)
                lg_done = P.cnt["ve"]
                for jh in range(2):
                    s, banks = slot_acquire()
                    slot_wait(s)
                    P.wait("tensor", "ve", lg_done)
                    for q in range(4):
                        j = jh * 4 + q
                        P.call("tensor", "transpose", banks[q][:, 0:128],
                               lg[:, j * 128:(j + 1) * 128], ident[:],
                               inc=[("pe", 1)] if q == 3 else None)
                    P.wait("vector", "pe", P.cnt["pe"])
                    for q in range(4):
                        P.call("vector", "tensor_copy", gtm[:, jh * 4 + q, :],
                               banks[q][:, 0:8],
                               inc=[("ve", 1)] if q == 3 else None)
                    slot_done(s)
                gate_tr_pe = P.cnt["pe"]
                P.wait("vector", "ve", c_gbt)
                P.wait("vector", "pe", gate_tr_pe)
                for j in range(JT):
                    g = gtm[:, j, :]
                    bia = lg[:, j * 8:(j + 1) * 8]
                    P.call("vector", "reduce_max", gst[:, 0:1], g, AXX)
                    P.call("vector", "tensor_scalar", gst[:, 0:1], gst[:, 0:1],
                           -1.0, None, A.mult, inc=[("ve", 1)])
                    P.wait("scalar", "ve", P.cnt["ve"])
                    P.call("scalar", "activation", g, g, AF.Exp,
                           bias=gst[:, 0:1], inc=[("se", 1)])
                    P.wait("vector", "se", P.cnt["se"])
                    P.call("vector", "reduce_sum", gst[:, 1:2], g, AXX)
                    P.call("vector", "reciprocal", gst[:, 1:2], gst[:, 1:2])
                    P.call("vector", "tensor_scalar", g, g, gst[:, 1:2], None,
                           A.mult)
                    P.call("vector", "tensor_add", bia, g, gbt[:])
                    P.call("vector", "reduce_max", gst[:, 2:3], bia, AXX)
                    P.call("vector", "tensor_scalar", cwb[:, 0:8], bia,
                           gst[:, 2:3], None, A.is_equal)
                    P.call("vector", "scalar_tensor_tensor", cwb[:, 8:16],
                           cwb[:, 0:8], -TOPK_BIG, bia, A.mult, A.add)
                    P.call("vector", "reduce_max", gst[:, 3:4], cwb[:, 8:16],
                           AXX)
                    P.call("vector", "tensor_scalar", cwb[:, 8:16],
                           cwb[:, 8:16], gst[:, 3:4], None, A.is_equal)
                    P.call("vector", "tensor_add", cwb[:, 0:8], cwb[:, 0:8],
                           cwb[:, 8:16])
                    P.call("vector", "tensor_mul", g, g, cwb[:, 0:8],
                           inc=[("ve", 1)])
                cw_done = P.cnt["ve"]
                for jh in range(2):
                    s, banks = slot_acquire()
                    slot_wait(s)
                    P.wait("tensor", "ve", cw_done)
                    for q in range(4):
                        j = jh * 4 + q
                        P.call("tensor", "transpose", banks[q][0:8, 0:128],
                               gtm[:, j, :], ident[:],
                               inc=[("pe", 1)] if q == 3 else None)
                    P.wait("vector", "pe", P.cnt["pe"])
                    for q in range(4):
                        j = jh * 4 + q
                        P.call("vector", "tensor_copy",
                               cwf[:, j * 128:(j + 1) * 128],
                               banks[q][0:8, 0:128],
                               inc=[("ve", 1)] if q == 3 else None)
                    slot_done(s)
                cwf_done = P.cnt["ve"]

                P.call("vector", "memset", acc[:], 0.0, inc=[("ve", 1)])

                for e in range(NE):
                    shared = (e == NE - 1)
                    y_done_prev = P.cnt["pe"]
                    if not shared:
                        s, banks = slot_acquire()
                        slot_wait(s)
                        P.wait("tensor", "ve", cwf_done)
                        for th in range(2):
                            P.call("tensor", "matmul", banks[th][:, :],
                                   sel8[:, e, :],
                                   cwf[:, th * 512:(th + 1) * 512],
                                   start=True, stop=True,
                                   inc=[("pe", 1)] if th == 1 else None)
                        P.wait("vector", "pe", P.cnt["pe"])
                        for th in range(2):
                            P.call("vector", "tensor_copy",
                                   cwb[:, th * 512:(th + 1) * 512],
                                   banks[th][:, :],
                                   inc=[("ve", 1)] if th == 1 else None)
                        slot_done(s)
                    P.wait("vector", "pe", y_done_prev)
                    for fb in range(8):
                        wb = fb % 2
                        wsem = ("sw0", "sw1")[wb]
                        if wf_cons[wb]:
                            P.wait("sync", "pe", wf_cons[wb])
                        wfl = P.dma("sync", wsem, wfs[:, wb],
                                    s_fw[e, fb].rearrange(
                                        "a p h c d -> p a h c d"))
                        for half in range(2):
                            pas = (0, 2) if half == 0 else (1, 3)
                            s, banks = slot_acquire()
                            slot_wait(s)
                            P.wait("tensor", "ve", x2_done)
                            P.wait("tensor", wsem, wfl)
                            nmm = 0
                            for pss in range(3):
                                mvx = (x2h, x2l, x2h)[pss]
                                hl = (0, 0, 1)[pss]
                                for cb in range(NCB):
                                    for qa in range(2):
                                        for th in range(2):
                                            nmm += 1
                                            P.call("tensor", "matmul",
                                                   banks[qa * 2 + th][:, :],
                                                   wfs[:, wb, pas[qa], hl, cb, :],
                                                   mvx[:, cb, th * 512:(th + 1) * 512],
                                                   start=(pss == 0 and cb == 0),
                                                   stop=(pss == 2 and cb == NCB - 1),
                                                   inc=[("pe", 1)] if nmm == 96 else None)
                            if half == 1:
                                wf_cons[wb] = P.cnt["pe"]
                            P.wait("scalar", "pe", P.cnt["pe"])
                            P.wait("scalar", "ve", P.cnt["ve"])
                            if half == 0:
                                for th in range(2):
                                    tsl = slice(th * 512, (th + 1) * 512)
                                    P.call("scalar", "activation", t1[:, tsl],
                                           banks[th][:, :], AF.Identity,
                                           bias=bvec[:, e, 0, fb:fb + 1])
                                    P.call("scalar", "activation", t3[:, tsl],
                                           banks[2 + th][:, :], AF.Identity,
                                           bias=bvec[:, e, 2, fb:fb + 1],
                                           inc=[("se", 1)] if th == 1 else None)
                                P.wait("vector", "se", P.cnt["se"])
                                P.call("vector", "tensor_mul", t1[:], t1[:], t3[:])
                                P.call("vector", "tensor_scalar", t1[:], t1[:],
                                       LIMIT, None, A.min, inc=[("ve", 1)])
                                P.wait("scalar", "ve", P.cnt["ve"])
                                P.call("scalar", "activation", t3[:], t1[:],
                                       AF.Silu, scale=ALPHA, inc=[("se", 1)])
                            else:
                                for th in range(2):
                                    tsl = slice(th * 512, (th + 1) * 512)
                                    P.call("scalar", "activation", t1[:, tsl],
                                           banks[th][:, :], AF.Identity,
                                           bias=bvec[:, e, 1, fb:fb + 1])
                                    P.call("scalar", "activation", lnt[:, tsl],
                                           banks[2 + th][:, :], AF.Identity,
                                           bias=bvec[:, e, 3, fb:fb + 1],
                                           inc=[("se", 1)] if th == 1 else None)
                                P.wait("vector", "se", P.cnt["se"])
                                P.call("vector", "tensor_mul", t1[:], t1[:], lnt[:])
                                P.call("vector", "tensor_scalar", t1[:], t1[:],
                                       LIMIT, -LIMIT, A.min, A.max)
                                P.call("vector", "tensor_scalar", t1[:], t1[:],
                                       1.0, 1.0 / ALPHA, A.add, A.mult)
                                P.call("vector", "tensor_mul", t1[:], t3[:], t1[:])
                                P.call("vector", "tensor_copy", ah[:, fb, :], t1[:])
                                P.call("vector", "tensor_sub", al[:, fb, :],
                                       t1[:], ah[:, fb, :], inc=[("ve", 1)])
                            slot_done(s)
                    a_done = P.cnt["ve"]

                    for dp in range(4):
                        wb = dp % 2
                        wsem = ("sw0", "sw1")[wb]
                        if w2_cons[wb]:
                            P.wait("sync", "pe", w2_cons[wb])
                        w2l = P.dma("sync", wsem, w2s[:, wb], s_fw2[e, dp])
                        s, banks = slot_acquire()
                        slot_wait(s)
                        P.wait("tensor", "ve", a_done)
                        P.wait("tensor", wsem, w2l)
                        nmm = 0
                        for fb in range(8):
                            for pss in range(3):
                                mva = (ah, al, ah)[pss]
                                hl = (0, 0, 1)[pss]
                                for dd in range(2):
                                    for th in range(2):
                                        nmm += 1
                                        P.call("tensor", "matmul",
                                               banks[dd * 2 + th][:, :],
                                               w2s[:, wb, hl, fb, dd, :],
                                               mva[:, fb, th * 512:(th + 1) * 512],
                                               start=(fb == 0 and pss == 0),
                                               stop=(fb == 7 and pss == 2),
                                               inc=[("pe", 1)] if nmm == 96 else None)
                        w2_cons[wb] = P.cnt["pe"]
                        P.wait("scalar", "pe", P.cnt["pe"])
                        for dd in range(2):
                            P.wait("scalar", "ve", P.cnt["ve"])
                            db = dp * 2 + dd
                            for th in range(2):
                                tsl = slice(th * 512, (th + 1) * 512)
                                P.call("scalar", "activation", t1[:, tsl],
                                       banks[dd * 2 + th][:, :], AF.Identity,
                                       bias=bvec[:, e, 4, db:db + 1],
                                       inc=[("se", 1)] if th == 1 else None)
                            P.wait("vector", "se", P.cnt["se"])
                            if shared:
                                P.call("vector", "tensor_add", acc[:, db, :],
                                       acc[:, db, :], t1[:], inc=[("ve", 1)])
                            else:
                                P.call("vector", "tensor_mul", t3[:], t1[:],
                                       cwb[:])
                                P.call("vector", "tensor_add", acc[:, db, :],
                                       acc[:, db, :], t3[:], inc=[("ve", 1)])
                        slot_done(s)

                acc_done = P.cnt["ve"]
                for db in range(8):
                    s, banks = slot_acquire()
                    slot_wait(s)
                    P.wait("tensor", "ve", acc_done)
                    for jh in range(2):
                        for q in range(4):
                            j = jh * 4 + q
                            P.call("tensor", "transpose",
                                   banks[jh * 2 + q // 2][:, (q % 2) * 128:(q % 2 + 1) * 128],
                                   acc[:, db, j * 128:(j + 1) * 128], ident[:],
                                   inc=[("pe", 1)] if (jh == 1 and q == 3) else None)
                    P.wait("vector", "pe", P.cnt["pe"])
                    for jh in range(2):
                        for q in range(4):
                            j = jh * 4 + q
                            P.call("vector", "tensor_add",
                                   xp_tm[:, j, db * 128:(db + 1) * 128],
                                   xp_tm[:, j, db * 128:(db + 1) * 128],
                                   banks[jh * 2 + q // 2][:, (q % 2) * 128:(q % 2 + 1) * 128],
                                   inc=[("ve", 1)] if (jh == 1 and q == 3) else None)
                    slot_done(s)
                fin_done = P.cnt["ve"]
                P.wait("gpsimd", "ve", fin_done)
                P.wait("gpsimd", "st", P.cnt["st"])
                P.call("gpsimd", "dma_start", inc=[("st", 16)],
                       out=d_out[tl * TT:(tl + 1) * TT, :].rearrange(
                           "(j p) c -> p j c", p=128),
                       in_=xp_tm[:])
                out_st = P.cnt["st"]

            P.wait("gpsimd", "st", P.cnt["st"])
            with nc.Block() as block:
                P.flush(block)

    return nc


def _prep_inputs(inp):
    f = lambda a: np.ascontiguousarray(np.asarray(a, dtype=np.float32))
    Wk, Wv, Wr, Wo = (f(inp["L0_attn_W" + s]) for s in "kvro")

    def slab_kxd(WT):
        return WT.reshape(8, 128, 8, 128).transpose(2, 1, 0, 3)

    attn_w = np.ascontiguousarray(np.stack(
        [slab_kxd(W.T) for W in (Wk, Wv, Wr, Wo)]))
    w1, w3, w2 = f(inp["L0_ffn_w1"]), f(inp["L0_ffn_w3"]), f(inp["L0_ffn_w2"])
    sw1, sw3, sw2 = f(inp["L0_ffn_sw1"]), f(inp["L0_ffn_sw3"]), f(inp["L0_ffn_sw2"])
    ffn_w = np.empty((NE, 4, 8, 128, 8, 128), np.float32)
    ffn_w2 = np.empty((NE, 4, 128, 8, 2, 128), np.float32)
    for e in range(NE):
        W1, W3, W2 = (w1[e], w3[e], w2[e]) if e < 8 else (sw1, sw3, sw2)
        for pa, M in enumerate((W1[0::2], W1[1::2], W3[0::2], W3[1::2])):
            ffn_w[e, pa] = M.T.reshape(8, 128, 8, 128).transpose(2, 1, 0, 3)
        ffn_w2[e] = W2.T.reshape(8, 128, 4, 2, 128).transpose(2, 1, 0, 3, 4)
    b1, b3, b2 = f(inp["L0_ffn_b1"]), f(inp["L0_ffn_b3"]), f(inp["L0_ffn_b2"])
    sb1, sb3, sb2 = f(inp["L0_ffn_sb1"]), f(inp["L0_ffn_sb3"]), f(inp["L0_ffn_sb2"])
    B = np.zeros((NE, 5, C), np.float32)
    for e in range(8):
        B[e, 0], B[e, 1] = b1[e, 0::2], b1[e, 1::2]
        B[e, 2], B[e, 3] = b3[e, 0::2], b3[e, 1::2]
        B[e, 4] = b2[e]
    B[8, 0], B[8, 1] = sb1[0::2], sb1[1::2]
    B[8, 2], B[8, 3] = sb3[0::2], sb3[1::2]
    B[8, 4] = sb2
    bias = np.ascontiguousarray(B.reshape(NE, 5, 8, 128).transpose(3, 0, 1, 2))
    vec = np.stack([
        f(inp["L0_attn_time_mix_k"]).reshape(C),
        f(inp["L0_attn_time_mix_v"]).reshape(C),
        f(inp["L0_attn_time_mix_r"]).reshape(C),
        f(inp["L0_attn_time_decay"]).reshape(C),
        f(inp["L0_attn_time_first"]).reshape(C),
        f(inp["L0_attn_ln_g"]).reshape(C),
        f(inp["L0_attn_ln_b"]).reshape(C),
    ])
    vec = np.ascontiguousarray(vec.reshape(7, 8, 128).transpose(2, 0, 1))
    gw = np.ascontiguousarray(
        f(inp["L0_ffn_gate_w"]).T.reshape(8, 128, 8).transpose(1, 0, 2))
    gb = f(inp["L0_ffn_gate_b"]).reshape(1, 8)
    return dict(attn_w=attn_w, ffn_w=np.ascontiguousarray(ffn_w),
                ffn_w2=np.ascontiguousarray(ffn_w2), gw=gw, vec=vec,
                bias=bias, gb=gb,
                ident=np.eye(128, dtype=np.float32),
                ones_row=np.ones((1, 128), np.float32),
                ones_col=np.ones((128, 1), np.float32),
                actc=np.tile(np.array([[LNEPS, -0.5, ALPHA, 1.0 / C]],
                                      np.float32), (128, 1)),
                sel8=np.ascontiguousarray(np.broadcast_to(
                    np.eye(8, dtype=np.float32)[:, :, None], (8, 8, 128))))


def kernel(**inp):
    x = np.ascontiguousarray(np.asarray(inp["L0_attn_x"], dtype=np.float32))
    B, T, _ = x.shape
    shared = _prep_inputs(inp)
    nc = build_nc(T=T)
    in_maps = [dict(shared, x=x[b]) for b in range(B)]
    res = run_bass_kernel_spmd(nc, in_maps, core_ids=list(range(B)))
    return np.stack([res.results[b]["out"] for b in range(B)])


if __name__ == "__main__":
    nc = build_nc(T=1024)
    print("instructions:", nc.n_instructions())


# version 05 (final)
# speedup vs baseline: 1.2043x; 1.2043x over previous
import numpy as np
import contextlib
import concourse.bass as bass
import concourse.mybir as mybir
from concourse.bass_utils import run_bass_kernel_spmd

F32 = mybir.dt.float32
BF16 = mybir.dt.bfloat16
A = mybir.AluOpType
AF = mybir.ActivationFunctionType
AXX = mybir.AxisListType.X

C = 1024
NCB = 8
TC = 512
JC = TC // 128
TT = 1024
JT = TT // 128
NE = 9
ALPHA = 1.702
LIMIT = 7.0
LNEPS = 1e-5
TOPK_BIG = 1e30

ENGS = ("sync", "gpsimd", "tensor", "vector", "scalar")
V_TMK, V_TMV, V_TMR, V_TD, V_TF, V_LNG, V_LNB = range(7)


def _flat(ap):
    n = 1
    for _, cnt in ap.ap[1:]:
        n *= cnt
    return bass.AP(ap.tensor, ap.offset, [list(ap.ap[0]), [1, n]])


class Prog:

    def __init__(self, sems):
        self.ops = {e: [] for e in ENGS}
        self.sems = sems
        self.cnt = {k: 0 for k in sems}

    def wait(self, eng, sem, val):
        if val <= 0:
            return
        assert val <= self.cnt[sem], (sem, val, self.cnt[sem])
        self.ops[eng].append(("wait", self.sems[sem], val))

    def call(self, eng, meth, *args, inc=None, **kw):
        incs = []
        if inc:
            for s, n in inc:
                self.cnt[s] += n
                incs.append((self.sems[s], n))
        self.ops[eng].append(("call", meth, args, kw, incs))
        if eng in ("vector", "scalar") and meth not in ("drain", "dma_start"):
            self.ops[eng].append(("call", "drain", (), {}, []))

    def dma(self, eng, sem, out, in_):
        self.wait(eng, sem, self.cnt[sem])
        self.call(eng, "dma_start", inc=[(sem, 16)], out=out, in_=in_)
        return self.cnt[sem]

    def flush(self, block):
        def run(e, ops):
            for op in ops:
                if op[0] == "wait":
                    e.wait_ge(op[1], op[2])
                else:
                    _, meth, args, kw, incs = op
                    ins = getattr(e, meth)(*args, **kw)
                    for s, n in incs:
                        ins = ins.then_inc(s, n)
        for name in ENGS:
            ops = self.ops[name]
            if ops:
                getattr(block, name)(lambda e, _o=ops: run(e, _o))
        self.ops = {e: [] for e in ENGS}

    def barrier_vals(self):
        return dict(self.cnt)

    def emit_barrier(self, vals):
        for e in ENGS:
            for s, v in vals.items():
                if v > 0:
                    self.wait(e, s, v)


def build_nc(T=4096):
    NCH = T // TC
    NTL = T // TT
    nc = bass.Bass("TRN2", target_bir_lowering=False, debug=False)

    dt_ = nc.dram_tensor
    d_x = dt_("x", [T, C], F32, kind="ExternalInput").ap()
    d_out = dt_("out", [T, C], F32, kind="ExternalOutput").ap()
    d_aw = dt_("attn_w", [4, 128, 8, NCB, 128], F32, kind="ExternalInput").ap()
    d_fw = dt_("ffn_w", [NE, 128, 8, 4, NCB, 128], F32, kind="ExternalInput").ap()
    d_fw2 = dt_("ffn_w2", [NE, 128, 4, 8, 2, 128], F32, kind="ExternalInput").ap()
    d_gw = dt_("gw", [128, NCB, 8], F32, kind="ExternalInput").ap()
    d_vec = dt_("vec", [128, 7, NCB], F32, kind="ExternalInput").ap()
    d_bias = dt_("bias", [128, NE, 5, 8], F32, kind="ExternalInput").ap()
    d_gb = dt_("gb", [1, 8], F32, kind="ExternalInput").ap()
    d_id = dt_("ident", [128, 128], F32, kind="ExternalInput").ap()
    d_or = dt_("ones_row", [1, 128], F32, kind="ExternalInput").ap()
    d_oc = dt_("ones_col", [128, 1], F32, kind="ExternalInput").ap()
    d_ac = dt_("actc", [128, 4], F32, kind="ExternalInput").ap()
    d_sel = dt_("sel8", [8, 8, 128], F32, kind="ExternalInput").ap()
    s_aw = dt_("aw_s", [4, 128, 8, 2, NCB, 128], BF16, kind="Internal").ap()
    s_fw = dt_("fw_s", [NE, 128, 8, 4, 2, NCB, 128], BF16, kind="Internal").ap()
    s_fw2 = dt_("fw2_s", [NE, 128, 4, 2, 8, 2, 128], BF16, kind="Internal").ap()
    s_xp = dt_("xpost", [T, C], F32, kind="Internal").ap()

    with contextlib.ExitStack() as ST:
        sem_names = ["sx0", "sx1", "sw0", "sw1", "sv", "ss0", "ss1", "st",
                     "pe", "ve", "se", "gp", "sp0", "sp1", "sq0", "sq1"]
        sems = {k: ST.enter_context(nc.semaphore(k)) for k in sem_names}
        P = Prog(sems)

        ident = ST.enter_context(nc.sbuf_tensor("identS", [128, 128], F32))
        ones_row = ST.enter_context(nc.sbuf_tensor("ones_rowS", [1, 128], F32))
        ones_col = ST.enter_context(nc.sbuf_tensor("ones_colS", [128, 1], F32))
        actc = ST.enter_context(nc.sbuf_tensor("actcS", [128, 4], F32))
        sel8 = ST.enter_context(nc.sbuf_tensor("sel8S", [8, 8, 128], F32))
        nc.const_aps.register(F32, LNEPS, actc[:, 0:1])
        nc.const_aps.register(F32, -0.5, actc[:, 1:2])
        nc.const_aps.register(F32, ALPHA, actc[:, 2:3])
        nc.const_aps.register(F32, 1.0 / C, actc[:, 3:4])
        vecs = ST.enter_context(nc.sbuf_tensor("vecs", [128, 7, NCB], F32))
        dec_sb = ST.enter_context(nc.sbuf_tensor("dec_sb", [128, NCB], F32))
        gws = ST.enter_context(nc.sbuf_tensor("gws", [128, NCB, 8], F32))
        gwh = ST.enter_context(nc.sbuf_tensor("gwh", [128, NCB, 8], BF16))
        gwl = ST.enter_context(nc.sbuf_tensor("gwl", [128, NCB, 8], BF16))
        gbt = ST.enter_context(nc.sbuf_tensor("gbt", [128, 8], F32))
        gbrow = ST.enter_context(nc.sbuf_tensor("gbrow", [1, 8], F32))
        bvec = ST.enter_context(nc.sbuf_tensor("bvec", [128, NE, 5, 8], F32))
        carry = ST.enter_context(nc.sbuf_tensor("carry", [128, NCB], F32))
        pb = [ST.enter_context(nc.psum_tensor("pb%d" % i, [128, 512], F32))
              for i in range(8)]

        slot_state = {"A": None, "B": None, "next": "A"}

        def slot_acquire():
            s = slot_state["next"]
            slot_state["next"] = "B" if s == "A" else "A"
            return s, (pb[0:4] if s == "A" else pb[4:8])

        def slot_wait(s):
            ev = slot_state[s]
            if ev is not None:
                P.wait("tensor", ev[0], ev[1])

        def slot_done(s, eng="ve"):
            slot_state[s] = (eng, P.cnt[eng])

        P.dma("sync", "sv", ident[:], d_id[:, :])
        P.dma("sync", "sv", ones_row[:], d_or[:, :])
        P.dma("sync", "sv", ones_col[:], d_oc[:, :])
        P.dma("sync", "sv", actc[:], d_ac[:, :])
        P.dma("sync", "sv", sel8[:], d_sel[:, :, :])
        P.dma("sync", "sv", vecs[:], d_vec[:, :, :])
        P.dma("sync", "sv", gws[:], d_gw[:, :, :])
        P.dma("sync", "sv", gbrow[:], d_gb[:, :])
        c_setup = P.dma("sync", "sv", bvec[:], d_bias[:, :, :, :])
        P.wait("scalar", "sv", c_setup)
        P.call("scalar", "activation", dec_sb[:], vecs[:, V_TD, :], AF.Exp,
               inc=[("se", 1)])
        P.wait("vector", "sv", c_setup)
        P.call("vector", "tensor_copy", gwh[:], gws[:])
        P.call("vector", "tensor_sub", gwl[:], gws[:], gwh[:], inc=[("ve", 1)])
        P.wait("tensor", "sv", c_setup)
        P.call("tensor", "matmul", pb[0][0:128, 0:8], ones_row[:], gbrow[:, :],
               start=True, stop=True, inc=[("pe", 1)])
        P.wait("vector", "pe", P.cnt["pe"])
        P.call("vector", "tensor_copy", gbt[:], pb[0][0:128, 0:8], inc=[("ve", 1)])
        c_gbt = P.cnt["ve"]

        NP0 = 8192
        with contextlib.ExitStack() as SB0:
            p0t = SB0.enter_context(nc.sbuf_tensor("p0_t", [128, 2, NP0], F32))
            p0h = SB0.enter_context(nc.sbuf_tensor("p0_h", [128, 2, NP0], BF16))
            p0l = SB0.enter_context(nc.sbuf_tensor("p0_l", [128, 2, NP0], BF16))
            slabs = []
            for pj in range(4):
                slabs.append((d_aw[pj], s_aw[pj][:, :, 0], s_aw[pj][:, :, 1],
                              8192))
            for e in range(NE):
                for fb in range(8):
                    slabs.append((d_fw[e][:, fb], s_fw[e][:, fb, :, 0],
                                  s_fw[e][:, fb, :, 1], 4096))
                for dp in range(4):
                    slabs.append((d_fw2[e][:, dp], s_fw2[e][:, dp, 0],
                                  s_fw2[e][:, dp, 1], 2048))
            cop = [0, 0]
            sub = [0, 0]
            stor = [0, 0]
            for i, (src, dh, dl, n) in enumerate(slabs):
                b = i % 2
                lsem = ("sp0", "sp1")[b]
                ssem = ("sq0", "sq1")[b]
                if sub[b]:
                    P.wait("sync", "ve", sub[b])
                if cop[b]:
                    P.wait("sync", "se", cop[b])
                ld = P.dma("sync", lsem, p0t[:, b, 0:n], src)
                P.wait("scalar", lsem, ld)
                if stor[b]:
                    P.wait("scalar", ssem, stor[b])
                P.call("scalar", "copy", p0h[:, b, 0:n], p0t[:, b, 0:n],
                       inc=[("se", 1)])
                cop[b] = P.cnt["se"]
                P.wait("vector", "se", cop[b])
                if stor[b]:
                    P.wait("vector", ssem, stor[b])
                P.call("vector", "tensor_sub", p0l[:, b, 0:n], p0t[:, b, 0:n],
                       p0h[:, b, 0:n], inc=[("ve", 1)])
                sub[b] = P.cnt["ve"]
                P.wait("gpsimd", "ve", sub[b])
                P.wait("gpsimd", ssem, P.cnt[ssem])
                P.call("gpsimd", "dma_start", inc=[(ssem, 16)], out=dh,
                       in_=p0h[:, b, 0:n])
                P.call("gpsimd", "dma_start", inc=[(ssem, 16)], out=dl,
                       in_=p0l[:, b, 0:n])
                stor[b] = P.cnt[ssem]
            bar0 = P.barrier_vals()
            with nc.Block() as block:
                P.flush(block)

        with contextlib.ExitStack() as SB:
            _t = lambda n, sh, dt: SB.enter_context(nc.sbuf_tensor(n, sh, dt))
            x_tm = _t("x_tm", [128, 2, JC, 1024], F32)
            sq = _t("sq", [128, 1024], F32)
            xn_fm = _t("xn_fm", [128, NCB, TC + 1], F32)
            dxm = _t("dxm", [128, NCB, TC], F32)
            xmix = _t("xmix", [128, NCB, TC], F32)
            mxh = _t("mxh", [128, NCB, TC], BF16)
            mxl = _t("mxl", [128, NCB, TC], BF16)
            k_fm = _t("k_fm", [128, NCB, TC], F32)
            v_fm = _t("v_fm", [128, NCB, TC], F32)
            r_fm = _t("r_fm", [128, NCB, TC], F32)
            t_fm = _t("t_fm", [128, NCB, TC], F32)
            stat = _t("stat", [128, 3], F32)
            rows = _t("rows", [1, 4, 512], F32)
            rb = _t("rb", [128, 512], F32)
            mrb = _t("mrb", [128, 512], F32)
            wsl = _t("wsl", [128, 2, 2, NCB, 128], BF16)
            P.emit_barrier(bar0)
            P.call("vector", "memset", xn_fm[:, :, 0:1], 0.0)
            P.call("vector", "memset", carry[:], 0.0, inc=[("ve", 1)])

            xld = [0, 0]
            xst = [0, 0]
            resid_done = [0, 0]
            wcons = [0, 0]
            slab_i = 0
            dxm_prev = 0
            ROW_M, ROW_MSQ, ROW_RSTD, ROW_MR = 0, 1, 2, 3

            for ch in range(NCH):
                cb_ = ch % 2
                xsem = ("sx0", "sx1")[cb_]
                ssem = ("ss0", "ss1")[cb_]
                P.wait("vector", "pe", P.cnt["pe"])
                if ch > 0:
                    P.wait("vector", "gp", dxm_prev)
                    P.call("vector", "tensor_copy", xn_fm[:, :, 0:1],
                           xn_fm[:, :, TC:TC + 1], inc=[("ve", 1)])
                if resid_done[cb_]:
                    P.wait("sync", "ve", resid_done[cb_])
                if xst[cb_]:
                    P.wait("sync", ssem, xst[cb_])
                xld[cb_] = P.dma("sync", xsem, x_tm[:, cb_, :, :],
                                 d_x[ch * TC:(ch + 1) * TC, :].rearrange(
                                     "(j p) c -> p j c", p=128))
                P.wait("vector", xsem, xld[cb_])
                P.wait("scalar", xsem, xld[cb_])
                tr_done_pe = 0
                halo_ve = P.cnt["ve"]
                for j in range(JC):
                    xt = x_tm[:, cb_, j, :]
                    P.call("vector", "reduce_sum", stat[:, 0:1], xt, AXX)
                    P.call("vector", "tensor_scalar", stat[:, 0:1], stat[:, 0:1],
                           -1.0 / C, None, A.mult, inc=[("ve", 1)])
                    if j > 0:
                        P.wait("scalar", "pe", tr_done_pe)
                    P.wait("scalar", "ve", P.cnt["ve"])
                    P.call("scalar", "activation", sq[:], xt, AF.Square,
                           bias=stat[:, 0:1], accum_out=stat[:, 1:2])
                    P.call("scalar", "activation", stat[:, 1:2], stat[:, 1:2],
                           AF.Ln, bias=LNEPS, scale=1.0 / C)
                    P.call("scalar", "activation", stat[:, 1:2], stat[:, 1:2],
                           AF.Exp, scale=-0.5, inc=[("se", 1)])
                    P.wait("vector", "se", P.cnt["se"])
                    P.call("vector", "tensor_mul", stat[:, 2:3], stat[:, 0:1],
                           stat[:, 1:2], inc=[("ve", 1)])
                    P.wait("scalar", "ve", P.cnt["ve"])
                    P.call("scalar", "activation", sq[:], xt, AF.Identity,
                           bias=stat[:, 2:3], scale=stat[:, 1:2],
                           inc=[("se", 1)])
                    s_, banks = slot_acquire()
                    slot_wait(s_)
                    P.wait("tensor", "se", P.cnt["se"])
                    if ch > 0 and j == 0:
                        P.wait("tensor", "gp", dxm_prev)
                        P.wait("tensor", "ve", halo_ve)
                    for cb in range(NCB):
                        P.call("tensor", "transpose",
                               banks[cb % 4][:, (cb // 4) * 128:(cb // 4) * 128 + 128],
                               sq[:, cb * 128:(cb + 1) * 128], ident[:],
                               inc=[("pe", 1)] if cb == NCB - 1 else None)
                    tr_done_pe = P.cnt["pe"]
                    P.wait("scalar", "pe", tr_done_pe)
                    for cb in range(NCB):
                        P.call("scalar", "copy",
                               xn_fm[:, cb, 1 + j * 128:1 + (j + 1) * 128],
                               banks[cb % 4][:, (cb // 4) * 128:(cb // 4) * 128 + 128],
                               inc=[("se", 1)] if cb == NCB - 1 else None)
                    slot_done(s_, "se")
                xn_evac_se = P.cnt["se"]
                P.wait("gpsimd", "se", xn_evac_se)
                P.wait("gpsimd", "ve", P.cnt["ve"])
                P.call("gpsimd", "tensor_sub", dxm[:], xn_fm[:, :, 1:TC + 1],
                       xn_fm[:, :, 0:TC], inc=[("gp", 1)])
                dxm_prev = P.cnt["gp"]

                ev_se = {}

                def mm_proj(pj, dest):
                    nonlocal slab_i
                    for half in range(2):
                        s_, banks = slot_acquire()
                        slot_wait(s_)
                        P.wait("tensor", "ve", P.cnt["ve"])
                        for q in range(4):
                            db = half * 4 + q
                            wb = slab_i % 2
                            wsem = ("sw0", "sw1")[wb]
                            if wcons[wb]:
                                P.wait("sync", "pe", wcons[wb])
                            wld = P.dma("sync", wsem, wsl[:, wb], s_aw[pj][:, db])
                            P.wait("tensor", wsem, wld)
                            nmm = 0
                            for pss in range(3):
                                mv = (mxh, mxl, mxh)[pss]
                                hl = (0, 0, 1)[pss]
                                for cb in range(NCB):
                                    nmm += 1
                                    P.call("tensor", "matmul", banks[q][:, :],
                                           wsl[:, wb, hl, cb, :], mv[:, cb, :],
                                           start=(nmm == 1), stop=(nmm == 24),
                                           inc=[("pe", 1)] if nmm == 24 else None)
                            wcons[wb] = P.cnt["pe"]
                            slab_i += 1
                        P.wait("scalar", "pe", P.cnt["pe"])
                        for q in range(4):
                            P.call("scalar", "copy",
                                   dest[:, half * 4 + q, :], banks[q][:, :],
                                   inc=[("se", 1)] if q == 3 else None)
                        slot_done(s_, "se")
                    ev_se[pj] = P.cnt["se"]

                for pj, dest in ((0, k_fm), (1, v_fm), (2, r_fm)):
                    P.wait("vector", "pe", P.cnt["pe"])
                    if pj == 0:
                        P.wait("vector", "gp", dxm_prev)
                    for cb in range(NCB):
                        P.call("vector", "scalar_tensor_tensor",
                               xmix[:, cb, :], dxm[:, cb, :],
                               vecs[:, pj, cb:cb + 1], xn_fm[:, cb, 0:TC],
                               A.mult, A.add)
                    P.call("vector", "tensor_copy", mxh[:], xmix[:])
                    P.call("vector", "tensor_sub", mxl[:], xmix[:], mxh[:],
                           inc=[("ve", 1)])
                    mm_proj(pj, dest)

                P.wait("gpsimd", "se", ev_se[1])
                P.call("gpsimd", "tensor_mul", k_fm[:], k_fm[:], v_fm[:],
                       inc=[("gp", 1)])
                kv_gp = P.cnt["gp"]
                P.wait("vector", "gp", kv_gp)
                for cb in range(NCB):
                    dbc = bass.AP(dec_sb, cb, [[NCB, 128], [0, TC]])
                    P.call("vector", "tensor_tensor_scan", v_fm[:, cb, :],
                           dbc, k_fm[:, cb, :], carry[:, cb:cb + 1],
                           A.mult, A.add)
                P.call("vector", "tensor_copy", carry[:], v_fm[:, :, TC - 1])
                for cb in range(NCB):
                    P.call("vector", "scalar_tensor_tensor", t_fm[:, cb, :],
                           k_fm[:, cb, :], vecs[:, V_TF, cb:cb + 1],
                           v_fm[:, cb, :], A.mult, A.add)
                P.wait("vector", "se", ev_se[2])
                P.call("vector", "tensor_mul", t_fm[:], r_fm[:], t_fm[:],
                       inc=[("ve", 1)])
                o_done = P.cnt["ve"]

                P.wait("scalar", "ve", o_done)
                P.call("scalar", "activation", k_fm[:], t_fm[:], AF.Square,
                       inc=[("se", 1)])
                s_, banks = slot_acquire()
                slot_wait(s_)
                P.wait("tensor", "ve", o_done)
                P.wait("tensor", "se", P.cnt["se"])
                for cb in range(NCB):
                    P.call("tensor", "matmul", banks[0][0:1, :], ones_col[:],
                           t_fm[:, cb, :], start=(cb == 0), stop=(cb == NCB - 1),
                           inc=[("pe", 1)] if cb == NCB - 1 else None)
                for cb in range(NCB):
                    P.call("tensor", "matmul", banks[1][0:1, :], ones_col[:],
                           k_fm[:, cb, :], start=(cb == 0), stop=(cb == NCB - 1),
                           inc=[("pe", 1)] if cb == NCB - 1 else None)
                P.wait("vector", "pe", P.cnt["pe"])
                P.call("vector", "tensor_copy", rows[:, ROW_M, :], banks[0][0:1, :])
                P.call("vector", "tensor_copy", rows[:, ROW_MSQ, :],
                       banks[1][0:1, :], inc=[("ve", 1)])
                slot_done(s_)
                P.call("vector", "tensor_scalar", rows[:, ROW_M, :],
                       rows[:, ROW_M, :], 1.0 / C, None, A.mult)
                P.call("vector", "tensor_mul", rows[:, ROW_RSTD, :],
                       rows[:, ROW_M, :], rows[:, ROW_M, :])
                P.call("vector", "scalar_tensor_tensor", rows[:, ROW_MSQ, :],
                       rows[:, ROW_MSQ, :], 1.0 / C, rows[:, ROW_RSTD, :],
                       A.mult, A.subtract, inc=[("ve", 1)])
                P.wait("scalar", "ve", P.cnt["ve"])
                P.call("scalar", "activation", rows[:, ROW_RSTD, :],
                       rows[:, ROW_MSQ, :], AF.Ln, bias=LNEPS)
                P.call("scalar", "activation", rows[:, ROW_RSTD, :],
                       rows[:, ROW_RSTD, :], AF.Exp, scale=-0.5,
                       inc=[("se", 1)])
                P.wait("vector", "se", P.cnt["se"])
                P.call("vector", "tensor_mul", rows[:, ROW_MR, :],
                       rows[:, ROW_M, :], rows[:, ROW_RSTD, :],
                       inc=[("ve", 1)])
                rows_done = P.cnt["ve"]
                s_, banks = slot_acquire()
                slot_wait(s_)
                P.wait("tensor", "ve", rows_done)
                P.call("tensor", "matmul", banks[0][:, :], ones_row[:],
                       rows[:, ROW_RSTD, :], start=True, stop=True)
                P.call("tensor", "matmul", banks[1][:, :], ones_row[:],
                       rows[:, ROW_MR, :], start=True, stop=True,
                       inc=[("pe", 1)])
                P.wait("scalar", "pe", P.cnt["pe"])
                P.call("scalar", "copy", rb[:], banks[0][:, :])
                P.call("scalar", "copy", mrb[:], banks[1][:, :],
                       inc=[("se", 1)])
                slot_done(s_, "se")
                rbmrb_se = P.cnt["se"]
                P.wait("vector", "pe", P.cnt["pe"])
                P.wait("vector", "se", rbmrb_se)
                for cb in range(NCB):
                    P.call("vector", "tensor_mul", xmix[:, cb, :],
                           t_fm[:, cb, :], rb[:])
                    P.call("vector", "tensor_sub", xmix[:, cb, :],
                           xmix[:, cb, :], mrb[:],
                           inc=[("ve", 1)] if cb == NCB - 1 else None)
                P.wait("scalar", "ve", P.cnt["ve"])
                P.wait("scalar", "pe", P.cnt["pe"])
                for cb in range(NCB):
                    P.call("scalar", "activation", xmix[:, cb, :],
                           xmix[:, cb, :], AF.Identity,
                           bias=vecs[:, V_LNB, cb:cb + 1],
                           scale=vecs[:, V_LNG, cb:cb + 1])
                P.call("scalar", "copy", mxh[:], xmix[:], inc=[("se", 1)])
                P.wait("vector", "se", P.cnt["se"])
                P.call("vector", "tensor_sub", mxl[:], xmix[:], mxh[:],
                       inc=[("ve", 1)])
                mm_proj(3, t_fm)
                ao_se = ev_se[3]
                for half in range(2):
                    s_, banks = slot_acquire()
                    slot_wait(s_)
                    P.wait("tensor", "se", ao_se)
                    for q in range(4):
                        db = half * 4 + q
                        for j in range(JC):
                            P.call("tensor", "transpose",
                                   banks[q][:, j * 128:(j + 1) * 128],
                                   t_fm[:, db, j * 128:(j + 1) * 128], ident[:],
                                   inc=[("pe", 1)] if (q == 3 and j == JC - 1) else None)
                    P.wait("vector", "pe", P.cnt["pe"])
                    for q in range(4):
                        db = half * 4 + q
                        for j in range(JC):
                            P.call("vector", "tensor_add",
                                   x_tm[:, cb_, j, db * 128:(db + 1) * 128],
                                   x_tm[:, cb_, j, db * 128:(db + 1) * 128],
                                   banks[q][:, j * 128:(j + 1) * 128],
                                   inc=[("ve", 1)] if (q == 3 and j == JC - 1) else None)
                    slot_done(s_)
                resid_done[cb_] = P.cnt["ve"]
                P.wait("gpsimd", "ve", resid_done[cb_])
                P.wait("gpsimd", ssem, P.cnt[ssem])
                P.call("gpsimd", "dma_start", inc=[(ssem, 16)],
                       out=s_xp[ch * TC:(ch + 1) * TC, :].rearrange(
                           "(j p) c -> p j c", p=128),
                       in_=x_tm[:, cb_, :, :])
                xst[cb_] = P.cnt[ssem]

            barA = P.barrier_vals()
            with nc.Block() as block:
                P.flush(block)

        with contextlib.ExitStack() as SB:
            _t = lambda n, sh, dt: SB.enter_context(nc.sbuf_tensor(n, sh, dt))
            xp_tm = _t("xp_tm", [128, JT, 1024], F32)
            acc = _t("acc", [128, 8, TT], F32)
            x2h = _t("x2h", [128, NCB, TT], BF16)
            x2l = _t("x2l", [128, NCB, TT], BF16)
            ah = _t("ah", [128, 8, TT], BF16)
            al = _t("al", [128, 8, TT], BF16)
            wfs = _t("wfs", [128, 2, 4, 2, NCB, 128], BF16)
            w2s = _t("w2s", [128, 2, 2, 8, 2, 128], BF16)
            lnt = _t("lnt", [128, 1024], F32)
            t1 = _t("t1", [128, 1024], F32)
            t3 = _t("t3", [128, 1024], F32)
            cwb = _t("cwb", [128, TT], F32)
            lg = _t("lg", [128, TT], F32)
            gtm = _t("gtm", [128, JT, 8], F32)
            gst = _t("gst", [128, 4], F32)
            cwf = _t("cwf", [8, TT], F32)
            bst = _t("bst", [128, 3], F32)
            P.emit_barrier(barA)
            P.call("vector", "memset", lg[:], 0.0, inc=[("ve", 1)])

            out_st = 0
            wf_cons = [0, 0]
            w2_cons = [0, 0]

            for tl in range(NTL):
                P.wait("vector", "pe", P.cnt["pe"])
                if out_st:
                    P.wait("sync", "st", out_st)
                xpld = P.dma("sync", "sx0", xp_tm[:],
                             s_xp[tl * TT:(tl + 1) * TT, :].rearrange(
                                 "(j p) c -> p j c", p=128))
                P.wait("vector", "sx0", xpld)
                P.wait("scalar", "sx0", xpld)
                tr_done_pe = 0
                for j in range(JT):
                    xt = xp_tm[:, j, :]
                    P.call("vector", "reduce_sum", bst[:, 0:1], xt, AXX)
                    P.call("vector", "tensor_scalar", bst[:, 0:1], bst[:, 0:1],
                           -1.0 / C, None, A.mult, inc=[("ve", 1)])
                    if j > 0:
                        P.wait("scalar", "pe", tr_done_pe)
                    P.wait("scalar", "ve", P.cnt["ve"])
                    P.call("scalar", "activation", lnt[:], xt, AF.Square,
                           bias=bst[:, 0:1], accum_out=bst[:, 1:2])
                    P.call("scalar", "activation", bst[:, 1:2], bst[:, 1:2],
                           AF.Ln, bias=LNEPS, scale=1.0 / C)
                    P.call("scalar", "activation", bst[:, 1:2], bst[:, 1:2],
                           AF.Exp, scale=-0.5, inc=[("se", 1)])
                    P.wait("vector", "se", P.cnt["se"])
                    P.call("vector", "tensor_mul", bst[:, 2:3], bst[:, 0:1],
                           bst[:, 1:2], inc=[("ve", 1)])
                    P.wait("scalar", "ve", P.cnt["ve"])
                    P.call("scalar", "activation", lnt[:], xt, AF.Identity,
                           bias=bst[:, 2:3], scale=bst[:, 1:2],
                           inc=[("se", 1)])
                    s, banks = slot_acquire()
                    slot_wait(s)
                    P.wait("tensor", "se", P.cnt["se"])
                    for cb in range(NCB):
                        P.call("tensor", "transpose",
                               banks[cb % 4][:, (cb // 4) * 128:(cb // 4) * 128 + 128],
                               lnt[:, cb * 128:(cb + 1) * 128], ident[:],
                               inc=[("pe", 1)] if cb == NCB - 1 else None)
                    tr_done_pe = P.cnt["pe"]
                    P.wait("vector", "pe", tr_done_pe)
                    for cb in range(NCB):
                        src = banks[cb % 4][:, (cb // 4) * 128:(cb // 4) * 128 + 128]
                        dst = slice(j * 128, (j + 1) * 128)
                        P.call("vector", "tensor_copy", x2h[:, cb, dst], src)
                        P.call("vector", "tensor_sub", x2l[:, cb, dst], src,
                               x2h[:, cb, dst],
                               inc=[("ve", 1)] if cb == NCB - 1 else None)
                    slot_done(s)
                x2_done = P.cnt["ve"]

                s, banks = slot_acquire()
                slot_wait(s)
                P.wait("tensor", "ve", x2_done)
                for th in range(2):
                    tsl = slice(th * 512, (th + 1) * 512)
                    nmm = 0
                    for pss in range(3):
                        mvx = (x2h, x2l, x2h)[pss]
                        gwx = (gwh, gwh, gwl)[pss]
                        for cb in range(NCB):
                            nmm += 1
                            P.call("tensor", "matmul", banks[th][0:8, :],
                                   gwx[:, cb, :], mvx[:, cb, tsl],
                                   start=(nmm == 1), stop=(nmm == 24),
                                   inc=[("pe", 1)] if nmm == 24 else None)
                P.wait("vector", "pe", P.cnt["pe"])
                for th in range(2):
                    P.call("vector", "tensor_copy",
                           lg[0:8, th * 512:(th + 1) * 512], banks[th][0:8, :],
                           inc=[("ve", 1)] if th == 1 else None)
                slot_done(s)
                lg_done = P.cnt["ve"]
                for jh in range(2):
                    s, banks = slot_acquire()
                    slot_wait(s)
                    P.wait("tensor", "ve", lg_done)
                    for q in range(4):
                        j = jh * 4 + q
                        P.call("tensor", "transpose", banks[q][:, 0:128],
                               lg[:, j * 128:(j + 1) * 128], ident[:],
                               inc=[("pe", 1)] if q == 3 else None)
                    P.wait("vector", "pe", P.cnt["pe"])
                    for q in range(4):
                        P.call("vector", "tensor_copy", gtm[:, jh * 4 + q, :],
                               banks[q][:, 0:8],
                               inc=[("ve", 1)] if q == 3 else None)
                    slot_done(s)
                gate_tr_pe = P.cnt["pe"]
                P.wait("vector", "ve", c_gbt)
                P.wait("vector", "pe", gate_tr_pe)
                for j in range(JT):
                    g = gtm[:, j, :]
                    bia = lg[:, j * 8:(j + 1) * 8]
                    P.call("vector", "reduce_max", gst[:, 0:1], g, AXX)
                    P.call("vector", "tensor_scalar", gst[:, 0:1], gst[:, 0:1],
                           -1.0, None, A.mult, inc=[("ve", 1)])
                    P.wait("scalar", "ve", P.cnt["ve"])
                    P.call("scalar", "activation", g, g, AF.Exp,
                           bias=gst[:, 0:1], inc=[("se", 1)])
                    P.wait("vector", "se", P.cnt["se"])
                    P.call("vector", "reduce_sum", gst[:, 1:2], g, AXX)
                    P.call("vector", "reciprocal", gst[:, 1:2], gst[:, 1:2])
                    P.call("vector", "tensor_scalar", g, g, gst[:, 1:2], None,
                           A.mult)
                    P.call("vector", "tensor_add", bia, g, gbt[:])
                    P.call("vector", "reduce_max", gst[:, 2:3], bia, AXX)
                    P.call("vector", "tensor_scalar", cwb[:, 0:8], bia,
                           gst[:, 2:3], None, A.is_equal)
                    P.call("vector", "scalar_tensor_tensor", cwb[:, 8:16],
                           cwb[:, 0:8], -TOPK_BIG, bia, A.mult, A.add)
                    P.call("vector", "reduce_max", gst[:, 3:4], cwb[:, 8:16],
                           AXX)
                    P.call("vector", "tensor_scalar", cwb[:, 8:16],
                           cwb[:, 8:16], gst[:, 3:4], None, A.is_equal)
                    P.call("vector", "tensor_add", cwb[:, 0:8], cwb[:, 0:8],
                           cwb[:, 8:16])
                    P.call("vector", "tensor_mul", g, g, cwb[:, 0:8],
                           inc=[("ve", 1)])
                cw_done = P.cnt["ve"]
                for jh in range(2):
                    s, banks = slot_acquire()
                    slot_wait(s)
                    P.wait("tensor", "ve", cw_done)
                    for q in range(4):
                        j = jh * 4 + q
                        P.call("tensor", "transpose", banks[q][0:8, 0:128],
                               gtm[:, j, :], ident[:],
                               inc=[("pe", 1)] if q == 3 else None)
                    P.wait("vector", "pe", P.cnt["pe"])
                    for q in range(4):
                        j = jh * 4 + q
                        P.call("vector", "tensor_copy",
                               cwf[:, j * 128:(j + 1) * 128],
                               banks[q][0:8, 0:128],
                               inc=[("ve", 1)] if q == 3 else None)
                    slot_done(s)
                cwf_done = P.cnt["ve"]

                P.call("vector", "memset", acc[:], 0.0, inc=[("ve", 1)])

                for e in range(NE):
                    shared = (e == NE - 1)
                    y_done_prev = P.cnt["pe"]
                    if not shared:
                        s, banks = slot_acquire()
                        slot_wait(s)
                        P.wait("tensor", "ve", cwf_done)
                        for th in range(2):
                            P.call("tensor", "matmul", banks[th][:, :],
                                   sel8[:, e, :],
                                   cwf[:, th * 512:(th + 1) * 512],
                                   start=True, stop=True,
                                   inc=[("pe", 1)] if th == 1 else None)
                        P.wait("vector", "pe", P.cnt["pe"])
                        for th in range(2):
                            P.call("vector", "tensor_copy",
                                   cwb[:, th * 512:(th + 1) * 512],
                                   banks[th][:, :],
                                   inc=[("ve", 1)] if th == 1 else None)
                        slot_done(s)
                    P.wait("vector", "pe", y_done_prev)
                    for fb in range(8):
                        wb = fb % 2
                        wsem = ("sw0", "sw1")[wb]
                        if wf_cons[wb]:
                            P.wait("sync", "pe", wf_cons[wb])
                        wfl = P.dma("sync", wsem, wfs[:, wb], s_fw[e][:, fb])
                        for half in range(2):
                            pas = (0, 2) if half == 0 else (1, 3)
                            s, banks = slot_acquire()
                            slot_wait(s)
                            P.wait("tensor", "ve", x2_done)
                            P.wait("tensor", wsem, wfl)
                            nmm = 0
                            for pss in range(3):
                                mvx = (x2h, x2l, x2h)[pss]
                                hl = (0, 0, 1)[pss]
                                for cb in range(NCB):
                                    for qa in range(2):
                                        for th in range(2):
                                            nmm += 1
                                            P.call("tensor", "matmul",
                                                   banks[qa * 2 + th][:, :],
                                                   wfs[:, wb, pas[qa], hl, cb, :],
                                                   mvx[:, cb, th * 512:(th + 1) * 512],
                                                   start=(pss == 0 and cb == 0),
                                                   stop=(pss == 2 and cb == NCB - 1),
                                                   inc=[("pe", 1)] if nmm == 96 else None)
                            if half == 1:
                                wf_cons[wb] = P.cnt["pe"]
                            P.wait("scalar", "pe", P.cnt["pe"])
                            P.wait("scalar", "ve", P.cnt["ve"])
                            if half == 0:
                                for th in range(2):
                                    tsl = slice(th * 512, (th + 1) * 512)
                                    P.call("scalar", "activation", t1[:, tsl],
                                           banks[th][:, :], AF.Identity,
                                           bias=bvec[:, e, 0, fb:fb + 1])
                                    P.call("scalar", "activation", t3[:, tsl],
                                           banks[2 + th][:, :], AF.Identity,
                                           bias=bvec[:, e, 2, fb:fb + 1],
                                           inc=[("se", 1)] if th == 1 else None)
                                P.wait("vector", "se", P.cnt["se"])
                                P.call("vector", "tensor_mul", t1[:], t1[:], t3[:])
                                P.call("vector", "tensor_scalar", t1[:], t1[:],
                                       LIMIT, None, A.min, inc=[("ve", 1)])
                                P.wait("scalar", "ve", P.cnt["ve"])
                                P.call("scalar", "activation", t3[:], t1[:],
                                       AF.Silu, scale=ALPHA, inc=[("se", 1)])
                            else:
                                for th in range(2):
                                    tsl = slice(th * 512, (th + 1) * 512)
                                    P.call("scalar", "activation", t1[:, tsl],
                                           banks[th][:, :], AF.Identity,
                                           bias=bvec[:, e, 1, fb:fb + 1])
                                    P.call("scalar", "activation", lnt[:, tsl],
                                           banks[2 + th][:, :], AF.Identity,
                                           bias=bvec[:, e, 3, fb:fb + 1],
                                           inc=[("se", 1)] if th == 1 else None)
                                P.wait("vector", "se", P.cnt["se"])
                                P.call("vector", "tensor_mul", t1[:], t1[:], lnt[:])
                                P.call("vector", "tensor_scalar", t1[:], t1[:],
                                       LIMIT, -LIMIT, A.min, A.max)
                                P.call("vector", "tensor_scalar", t1[:], t1[:],
                                       1.0, 1.0 / ALPHA, A.add, A.mult)
                                P.call("vector", "tensor_mul", t1[:], t3[:], t1[:])
                                P.call("vector", "tensor_copy", ah[:, fb, :], t1[:])
                                P.call("vector", "tensor_sub", al[:, fb, :],
                                       t1[:], ah[:, fb, :], inc=[("ve", 1)])
                            slot_done(s)
                    a_done = P.cnt["ve"]

                    for dp in range(4):
                        wb = dp % 2
                        wsem = ("sw0", "sw1")[wb]
                        if w2_cons[wb]:
                            P.wait("sync", "pe", w2_cons[wb])
                        w2l = P.dma("sync", wsem, w2s[:, wb], s_fw2[e][:, dp])
                        s, banks = slot_acquire()
                        slot_wait(s)
                        P.wait("tensor", "ve", a_done)
                        P.wait("tensor", wsem, w2l)
                        nmm = 0
                        for fb in range(8):
                            for pss in range(3):
                                mva = (ah, al, ah)[pss]
                                hl = (0, 0, 1)[pss]
                                for dd in range(2):
                                    for th in range(2):
                                        nmm += 1
                                        P.call("tensor", "matmul",
                                               banks[dd * 2 + th][:, :],
                                               w2s[:, wb, hl, fb, dd, :],
                                               mva[:, fb, th * 512:(th + 1) * 512],
                                               start=(fb == 0 and pss == 0),
                                               stop=(fb == 7 and pss == 2),
                                               inc=[("pe", 1)] if nmm == 96 else None)
                        w2_cons[wb] = P.cnt["pe"]
                        P.wait("scalar", "pe", P.cnt["pe"])
                        for dd in range(2):
                            P.wait("scalar", "ve", P.cnt["ve"])
                            db = dp * 2 + dd
                            for th in range(2):
                                tsl = slice(th * 512, (th + 1) * 512)
                                P.call("scalar", "activation", t1[:, tsl],
                                       banks[dd * 2 + th][:, :], AF.Identity,
                                       bias=bvec[:, e, 4, db:db + 1],
                                       inc=[("se", 1)] if th == 1 else None)
                            P.wait("vector", "se", P.cnt["se"])
                            if shared:
                                P.call("vector", "tensor_add", acc[:, db, :],
                                       acc[:, db, :], t1[:], inc=[("ve", 1)])
                            else:
                                P.call("vector", "tensor_mul", t3[:], t1[:],
                                       cwb[:])
                                P.call("vector", "tensor_add", acc[:, db, :],
                                       acc[:, db, :], t3[:], inc=[("ve", 1)])
                        slot_done(s)

                acc_done = P.cnt["ve"]
                for db in range(8):
                    s, banks = slot_acquire()
                    slot_wait(s)
                    P.wait("tensor", "ve", acc_done)
                    for jh in range(2):
                        for q in range(4):
                            j = jh * 4 + q
                            P.call("tensor", "transpose",
                                   banks[jh * 2 + q // 2][:, (q % 2) * 128:(q % 2 + 1) * 128],
                                   acc[:, db, j * 128:(j + 1) * 128], ident[:],
                                   inc=[("pe", 1)] if (jh == 1 and q == 3) else None)
                    P.wait("vector", "pe", P.cnt["pe"])
                    for jh in range(2):
                        for q in range(4):
                            j = jh * 4 + q
                            P.call("vector", "tensor_add",
                                   xp_tm[:, j, db * 128:(db + 1) * 128],
                                   xp_tm[:, j, db * 128:(db + 1) * 128],
                                   banks[jh * 2 + q // 2][:, (q % 2) * 128:(q % 2 + 1) * 128],
                                   inc=[("ve", 1)] if (jh == 1 and q == 3) else None)
                    slot_done(s)
                fin_done = P.cnt["ve"]
                P.wait("gpsimd", "ve", fin_done)
                P.wait("gpsimd", "st", P.cnt["st"])
                P.call("gpsimd", "dma_start", inc=[("st", 16)],
                       out=d_out[tl * TT:(tl + 1) * TT, :].rearrange(
                           "(j p) c -> p j c", p=128),
                       in_=xp_tm[:])
                out_st = P.cnt["st"]

            P.wait("gpsimd", "st", P.cnt["st"])
            with nc.Block() as block:
                P.flush(block)

    return nc


def _prep_inputs(inp):
    f = lambda a: np.ascontiguousarray(np.asarray(a, dtype=np.float32))
    Wk, Wv, Wr, Wo = (f(inp["L0_attn_W" + s]) for s in "kvro")

    def slab_kxd(WT):
        return WT.reshape(8, 128, 8, 128).transpose(1, 2, 0, 3)

    attn_w = np.ascontiguousarray(np.stack(
        [slab_kxd(W.T) for W in (Wk, Wv, Wr, Wo)]))
    w1, w3, w2 = f(inp["L0_ffn_w1"]), f(inp["L0_ffn_w3"]), f(inp["L0_ffn_w2"])
    sw1, sw3, sw2 = f(inp["L0_ffn_sw1"]), f(inp["L0_ffn_sw3"]), f(inp["L0_ffn_sw2"])
    ffn_w = np.empty((NE, 128, 8, 4, 8, 128), np.float32)
    ffn_w2 = np.empty((NE, 128, 4, 8, 2, 128), np.float32)
    for e in range(NE):
        W1, W3, W2 = (w1[e], w3[e], w2[e]) if e < 8 else (sw1, sw3, sw2)
        for pa, M in enumerate((W1[0::2], W1[1::2], W3[0::2], W3[1::2])):
            ffn_w[e, :, :, pa] = M.T.reshape(8, 128, 8, 128).transpose(
                1, 2, 0, 3)
        ffn_w2[e] = W2.T.reshape(8, 128, 4, 2, 128).transpose(1, 2, 0, 3, 4)
    b1, b3, b2 = f(inp["L0_ffn_b1"]), f(inp["L0_ffn_b3"]), f(inp["L0_ffn_b2"])
    sb1, sb3, sb2 = f(inp["L0_ffn_sb1"]), f(inp["L0_ffn_sb3"]), f(inp["L0_ffn_sb2"])
    B = np.zeros((NE, 5, C), np.float32)
    for e in range(8):
        B[e, 0], B[e, 1] = b1[e, 0::2], b1[e, 1::2]
        B[e, 2], B[e, 3] = b3[e, 0::2], b3[e, 1::2]
        B[e, 4] = b2[e]
    B[8, 0], B[8, 1] = sb1[0::2], sb1[1::2]
    B[8, 2], B[8, 3] = sb3[0::2], sb3[1::2]
    B[8, 4] = sb2
    bias = np.ascontiguousarray(B.reshape(NE, 5, 8, 128).transpose(3, 0, 1, 2))
    vec = np.stack([
        f(inp["L0_attn_time_mix_k"]).reshape(C),
        f(inp["L0_attn_time_mix_v"]).reshape(C),
        f(inp["L0_attn_time_mix_r"]).reshape(C),
        f(inp["L0_attn_time_decay"]).reshape(C),
        f(inp["L0_attn_time_first"]).reshape(C),
        f(inp["L0_attn_ln_g"]).reshape(C),
        f(inp["L0_attn_ln_b"]).reshape(C),
    ])
    vec = np.ascontiguousarray(vec.reshape(7, 8, 128).transpose(2, 0, 1))
    gw = np.ascontiguousarray(
        f(inp["L0_ffn_gate_w"]).T.reshape(8, 128, 8).transpose(1, 0, 2))
    gb = f(inp["L0_ffn_gate_b"]).reshape(1, 8)
    return dict(attn_w=attn_w, ffn_w=np.ascontiguousarray(ffn_w),
                ffn_w2=np.ascontiguousarray(ffn_w2), gw=gw, vec=vec,
                bias=bias, gb=gb,
                ident=np.eye(128, dtype=np.float32),
                ones_row=np.ones((1, 128), np.float32),
                ones_col=np.ones((128, 1), np.float32),
                actc=np.tile(np.array([[LNEPS, -0.5, ALPHA, 1.0 / C]],
                                      np.float32), (128, 1)),
                sel8=np.ascontiguousarray(np.broadcast_to(
                    np.eye(8, dtype=np.float32)[:, :, None], (8, 8, 128))))


def kernel(**inp):
    x = np.ascontiguousarray(np.asarray(inp["L0_attn_x"], dtype=np.float32))
    B, T, _ = x.shape
    shared = _prep_inputs(inp)
    nc = build_nc(T=T)
    in_maps = [dict(shared, x=x[b]) for b in range(B)]
    res = run_bass_kernel_spmd(nc, in_maps, core_ids=list(range(B)))
    return np.stack([res.results[b]["out"] for b in range(B)])


if __name__ == "__main__":
    nc = build_nc(T=1024)
    print("instructions:", nc.n_instructions())
